# version 05
# baseline (speedup 1.0000x reference)
import numpy as np
import concourse.bass as bass
import concourse.mybir as mybir
from concourse.bass_utils import run_bass_kernel_spmd
from contextlib import ExitStack

F32 = mybir.dt.float32
BF16 = mybir.dt.bfloat16
AF = mybir.ActivationFunctionType
ALU = mybir.AluOpType
AX = mybir.AxisListType

NSEQ = 2
T = 2048
C = 1024
R = 128
E = 8
FF = 4096
NTC = T // 128
STRIPE = 1024
NSTR = T // STRIPE
EPS = 1e-5

DEBUG = False


class Sem:
    def __init__(self, h):
        self.h = h
        self.n = 0

    def bump(self, k=1):
        self.n += k
        return self.n


class Prog:
    def __init__(self, nc, es):
        self.nc = nc
        self.es = es
        self.streams = {k: [] for k in ("tensor", "vector", "scalar", "sync", "gpsimd")}
        self.ctr = 0
        self.marks = {}
        self.labels = []

    def sem(self, name):
        return Sem(self.es.enter_context(self.nc.semaphore(name)))

    def emit(self, eng, fn):
        self.ctr += 1
        self.streams[eng].append((self.ctr, fn))

    def mark(self, name):
        self.marks[name] = self.ctr


import inspect


def _op(P, eng, make, waits=(), inc=None, inck=1):
    P.labels.append((P.ctr + 1, eng, inspect.stack(0)[1].lineno))
    incs = []
    if inc is not None:
        if not isinstance(inc, list):
            inc = [(inc, inck)]
        for sem_, k_ in inc:
            sem_.bump(k_)
            incs.append((sem_.h, k_))
    wl = [(s.h, v) for (s, v) in waits if v > 0]

    def f(eng_obj):
        for h, v in wl:
            eng_obj.wait_ge(h, v)
        ins = make(eng_obj)
        for h, k_ in incs:
            ins.then_inc(h, k_)
    P.emit(eng, f)


build_info = {}


def build_program(debug=False, upto=None):
    nc = bass.Bass("TRN2", target_bir_lowering=False)

    d_x = nc.dram_tensor("x", [NSEQ, T, C], F32, kind="ExternalInput").ap()
    d_kvd = nc.dram_tensor("w_kv_down", [C, R], F32, kind="ExternalInput").ap()
    d_kvu = nc.dram_tensor("w_kv_up", [R, 2048], F32, kind="ExternalInput").ap()
    d_wq = nc.dram_tensor("w_q", [C, C], F32, kind="ExternalInput").ap()
    d_wo = nc.dram_tensor("w_out", [C, C], F32, kind="ExternalInput").ap()
    d_rw = nc.dram_tensor("router_wt", [C, E], F32, kind="ExternalInput").ap()
    d_w1 = nc.dram_tensor("w1t", [E, C, FF], F32, kind="ExternalInput").ap()
    d_w2 = nc.dram_tensor("w2t", [E, FF, C], F32, kind="ExternalInput").ap()
    d_b1 = nc.dram_tensor("b1t", [E, 128, 32], F32, kind="ExternalInput").ap()
    d_b2 = nc.dram_tensor("b2", [E, C], F32, kind="ExternalInput").ap()
    d_id = nc.dram_tensor("ident", [128, 128], F32, kind="ExternalInput").ap()
    d_mask = nc.dram_tensor("mask_tri", [128, 128], F32, kind="ExternalInput").ap()
    d_out = nc.dram_tensor("out", [NSEQ, T, C], F32, kind="ExternalOutput").ap()

    kscr = "ExternalOutput" if debug else "Internal"
    s_kvd = nc.dram_tensor("s_kvd", [C, R], BF16, kind="Internal").ap()
    s_kvu = nc.dram_tensor("s_kvu", [R, 2048], BF16, kind="Internal").ap()
    s_wq = nc.dram_tensor("s_wq", [C, C], BF16, kind="Internal").ap()
    s_wo = nc.dram_tensor("s_wo", [C, C], BF16, kind="Internal").ap()
    s_w1 = nc.dram_tensor("s_w1", [E, C, FF], BF16, kind="Internal").ap()
    s_w2 = nc.dram_tensor("s_w2", [E, FF, C], BF16, kind="Internal").ap()
    s_xn = nc.dram_tensor("s_xn", [NSEQ, 8, 128, T], BF16, kind=kscr).ap()
    s_acc = nc.dram_tensor("s_acc", [NSEQ, T, C], F32, kind=kscr).ap()
    if debug:
        d_gtok = nc.dram_tensor("g_tok_dbg", [NSEQ, 128, NTC * 8], F32,
                                kind="ExternalOutput").ap()

    SB_BASE = 16512
    SB_TOP = 229376
    off = [SB_BASE]

    def at(name, shape, dtype, o=None):
        nbytes = int(np.prod(shape[1:])) * (2 if dtype == BF16 else 4)
        if o is None:
            o = off[0]
            off[0] = (o + nbytes + 31) & ~31
        return nc.alloc_sbuf_tensor_at(name, shape, dtype, offset=o).ap()

    ident32 = at("ident32", [128, 128], F32)
    ident_bf = at("ident_bf", [128, 128], BF16)
    mask_sb = at("mask_sb", [128, 128], F32)
    ones_bf = at("ones_bf", [128, 64], BF16)
    router_sb = at("router_sb", [128, 64], F32)
    g_tok = at("g_tok", [128, NTC * 8], F32)
    g_exp = at("g_exp", [8, T], BF16)
    b2_bf = at("b2_bf", [8, C], BF16)
    b1_cur = at("b1_cur", [128, 64], F32)
    stats = at("stats", [128, 16], F32)
    gsc = at("gsc", [128, 48], F32)
    gbf = at("gbf", [128, 8], BF16)
    RB = (off[0] + 31) & ~31

    off[0] = RB
    x_rot = [at(f"x_rot{i}", [128, 1024], F32) for i in range(2)]
    xn32_rot = [at(f"xn32_{i}", [128, 1024], F32) for i in range(2)]
    xnT32_rot = [at(f"xnT32_{i}", [128, 1024], F32) for i in range(2)]
    junk_sq = at("junk_sq", [128, 1024], F32)
    xnT_base = off[0]
    xnT_sb = at("xnT_sb", [128, 8 * T], BF16)
    yT_all = at("yT_all", [128, 8 * T], BF16)
    latentT = at("latentT", [128, T], BF16)
    kT_rot = [at(f"kT_{i}", [128, T], BF16) for i in range(2)]
    qT_rot = [at(f"qT_{i}", [128, T], BF16) for i in range(2)]
    v_rot = [at(f"v_{i}", [128, NTC * 130], BF16) for i in range(2)]
    pt_rot = [at(f"pt_{i}", [128, 1024], BF16) for i in range(2)]
    kvd_sb = at("kvd_sb", [128, 1024], BF16)
    kvu_sb = at("kvu_sb", [128, 2048], BF16)
    wq_sb = at("wq_sb", [128, 8192], BF16)
    wo_sb = at("wo_sb", [128, 8192], BF16)
    rl_bf = at("rl_bf", [128, 512], BF16)
    rlb32 = at("rlb32", [64, 512], F32)
    a_end = off[0]
    off[0] = xnT_base
    b2_32 = at("b2_32", [8, C], F32)
    off[0] = RB
    w2_sb = at("w2_sb", [128, 32 * 1024], BF16)
    w1_ring = at("w1_ring", [128, 8 * 1024], BF16)
    h1T = at("h1T", [128, 32 * STRIPE], BF16)
    acc_sb = at("acc_sb", [128, 8 * 1024], F32)
    xn_str = at("xn_str", [128, 8 * STRIPE], BF16)
    b_end = off[0]
    stage32 = [at(f"stage32_{i}", [128, 512], F32, o=b_end + 2048 * i)
               for i in range(2)]
    stagebf = [at(f"stagebf_{i}", [128, 512], BF16, o=b_end + 4096 + 1024 * i)
               for i in range(2)]
    assert b_end + 6144 <= SB_TOP and a_end <= SB_TOP, (a_end, b_end)

    es_top = ExitStack()
    with es_top:
        P = Prog(nc, es_top)
        P01 = es_top.enter_context(nc.psum_tensor("P01", [128, 1024], F32))[:]
        P23 = es_top.enter_context(nc.psum_tensor("P23", [128, 1024], F32))[:]
        P45 = es_top.enter_context(nc.psum_tensor("P45", [128, 1024], F32))[:]
        P67 = es_top.enter_context(nc.psum_tensor("P67", [128, 1024], F32))[:]

        sm = {name: P.sem(name) for name in [
            "ld_pre", "ld_misc", "ld_b2", "ld_x", "ld_attnw", "ld_xn", "ld_acc",
            "ld_w1", "ld_w2", "ld_b1",
            "st_wbf", "st_xnT", "st_acc", "st_out",
            "cast", "tr", "xnT", "xnT32", "rmm", "gv1", "gs1", "gv2", "gtr", "gexp",
            "sq", "vstat", "sstat", "xn",
            "prj", "prjE", "qk", "msk", "expo", "pv", "rl", "rlb", "rlbe", "ynorm",
            "wo", "accv",
            "h1f", "gelu", "h2mm", "hev", "vch", "vmem",
        ]}

        _op(P, "sync", lambda e: e.dma_start(out=ident32, in_=d_id),
            inc=sm["ld_misc"], inck=16)
        _op(P, "sync", lambda e: e.dma_start(out=mask_sb, in_=d_mask),
            inc=sm["ld_misc"], inck=16)
        _op(P, "sync", lambda e: e.dma_start(
            out=router_sb.rearrange("p (c e2) -> p c e2", e2=8),
            in_=d_rw.rearrange("(c p) e2 -> p c e2", p=128)),
            inc=sm["ld_misc"], inck=16)
        _op(P, "sync", lambda e: e.dma_start(out=b2_32, in_=d_b2),
            inc=sm["ld_b2"], inck=16)

        _op(P, "scalar", lambda e: e.copy(out=ident_bf, in_=ident32),
            waits=[(sm["ld_misc"], 48)], inc=sm["cast"])
        _op(P, "vector", lambda e: e.memset(ones_bf, 1.0), inc=sm["cast"])
        _op(P, "gpsimd", lambda e: e.tensor_copy(out=b2_bf, in_=b2_32),
            waits=[(sm["ld_b2"], 16)], inc=sm["cast"])
        CAST0 = 3

        cast_jobs = []
        def add_cast_jobs(src, dst, rows, cols):
            cw = min(512, cols)
            for r0 in range(0, rows, 128):
                for c0 in range(0, cols, cw):
                    cast_jobs.append((src[r0:r0 + 128, c0:c0 + cw],
                                      dst[r0:r0 + 128, c0:c0 + cw], cw))

        add_cast_jobs(d_kvd, s_kvd, C, R)
        add_cast_jobs(d_kvu, s_kvu, R, 2048)
        add_cast_jobs(d_wq, s_wq, C, C)
        add_cast_jobs(d_wo, s_wo, C, C)
        n_attn_jobs = len(cast_jobs)
        w1_done_idx = []
        w2_done_idx = []
        for e_ in range(E):
            add_cast_jobs(d_w1[e_], s_w1[e_], C, FF)
            w1_done_idx.append(len(cast_jobs))
            add_cast_jobs(d_w2[e_], s_w2[e_], FF, C)
            w2_done_idx.append(len(cast_jobs))
        njobs = len(cast_jobs)

        drip_jobs = []
        for j, (srcap, dstap, cw) in enumerate(cast_jobs):
            def mk_load(j=j, srcap=srcap, cw=cw):
                _op(P, "sync", lambda e, s=srcap, b=stage32[j % 2], w=cw:
                    e.dma_start(out=b[:, 0:w], in_=s),
                    waits=[(sm["cast"], CAST0 + max(0, j - 1))],
                    inc=sm["ld_pre"], inck=16)
            drip_jobs.append(mk_load)
            _op(P, "gpsimd", lambda e, b32=stage32[j % 2], bbf=stagebf[j % 2], w=cw:
                e.tensor_copy(out=bbf[:, 0:w], in_=b32[:, 0:w]),
                waits=[(sm["ld_pre"], 16 * (j + 1)),
                       (sm["st_wbf"], 16 * max(0, j - 1))],
                inc=sm["cast"])
            _op(P, "gpsimd", lambda e, d=dstap, bbf=stagebf[j % 2], w=cw:
                e.dma_start(out=d, in_=bbf[:, 0:w]),
                waits=[(sm["cast"], CAST0 + j + 1)],
                inc=sm["st_wbf"], inck=16)
        drip_state = [0]

        def drip(n):
            a = drip_state[0]
            b = min(a + n, len(drip_jobs))
            for i in range(a, b):
                drip_jobs[i]()
            drip_state[0] = b
        wbf_attn_done = 16 * n_attn_jobs
        wbf_w1_done = [16 * i for i in w1_done_idx]
        wbf_w2_done = [16 * i for i in w2_done_idx]

        glob_slab = [0]
        ldx_ct = [0]

        def seq_phase_A(s):
            tra = P01[:, 0:512]
            trb = P01[:, 512:1024]
            psr = P23[:, 0:512]
            psg = P23[:, 512:1024].bitcast(BF16)

            for t in range(NTC):
                ldx_ct[0] += 1
                myx = 16 * ldx_ct[0]
                w = [(sm["xn"], max(0, sm["xn"].n - 1)),
                     (sm["sq"], max(0, sm["sq"].n - 1)),
                     (sm["st_acc"], 16 * 16 * s)]
                if t == 0:
                    w += [(sm["cast"], CAST0),
                          (sm["st_out"], 16 * NSTR * s),
                          (sm["h2mm"], sm["h2mm"].n)]
                _op(P, "sync", lambda e, t=t: e.dma_start(
                    out=x_rot[t % 2], in_=d_x[s, t * 128:(t + 1) * 128, :]),
                    waits=w, inc=sm["ld_x"], inck=16)
                drip(8)
                so = 8 * (t % 2)
                sqw = [(sm["ld_x"], myx), (sm["xn"], max(0, sm["xn"].n - 1))]
                if t == 0:
                    sqw.append((sm["h2mm"], sm["h2mm"].n))
                _op(P, "scalar", lambda e, t=t, so=so: e.activation(
                    out=junk_sq, in_=x_rot[t % 2], func=AF.Square,
                    accum_out=stats[:, so + 0:so + 1]),
                    waits=sqw, inc=sm["sq"])
                _op(P, "vector", lambda e, t=t, so=so: e.reduce_sum(
                    out=stats[:, so + 1:so + 2], in_=x_rot[t % 2], axis=AX.X),
                    waits=[(sm["ld_x"], myx)], inc=sm["vch"])
                _op(P, "vector", lambda e, so=so: e.tensor_scalar_mul(
                    out=stats[:, so + 2:so + 3], in0=stats[:, so + 1:so + 2],
                    scalar1=1.0 / C),
                    waits=[(sm["vch"], sm["vch"].n)], inc=sm["vch"])
                _op(P, "vector", lambda e, so=so: e.tensor_tensor(
                    out=stats[:, so + 3:so + 4], in0=stats[:, so + 2:so + 3],
                    in1=stats[:, so + 2:so + 3], op=ALU.mult),
                    waits=[(sm["vch"], sm["vch"].n)], inc=sm["vch"])
                _op(P, "vector", lambda e, so=so: e.tensor_scalar_mul(
                    out=stats[:, so + 4:so + 5], in0=stats[:, so + 0:so + 1],
                    scalar1=1.0 / C),
                    waits=[(sm["sq"], sm["sq"].n)], inc=sm["vch"])
                _op(P, "vector", lambda e, so=so: e.tensor_tensor(
                    out=stats[:, so + 5:so + 6], in0=stats[:, so + 4:so + 5],
                    in1=stats[:, so + 3:so + 4], op=ALU.subtract),
                    waits=[(sm["vch"], sm["vch"].n)], inc=sm["vch"])
                _op(P, "vector", lambda e, so=so: e.tensor_scalar_add(
                    out=stats[:, so + 6:so + 7], in0=stats[:, so + 5:so + 6],
                    scalar1=EPS),
                    waits=[(sm["vch"], sm["vch"].n)], inc=sm["vch"])
                _op(P, "vector", lambda e, so=so: e.reciprocal(
                    out=stats[:, so + 7:so + 8], in_=stats[:, so + 6:so + 7]),
                    waits=[(sm["vch"], sm["vch"].n)], inc=sm["vstat"])
                _op(P, "scalar", lambda e, so=so: e.activation(
                    out=stats[:, so + 6:so + 7], in_=stats[:, so + 7:so + 8],
                    func=AF.Sqrt),
                    waits=[(sm["vstat"], sm["vstat"].n)], inc=sm["sstat"])
                _op(P, "vector", lambda e, t=t, so=so: e.tensor_scalar(
                    out=xn32_rot[t % 2], in0=x_rot[t % 2],
                    scalar1=stats[:, so + 2:so + 3], scalar2=stats[:, so + 6:so + 7],
                    op0=ALU.subtract, op1=ALU.mult),
                    waits=[(sm["sstat"], sm["sstat"].n)],
                    inc=sm["xn"])
                for half, trp in ((0, tra), (1, trb)):
                    for cc in range(4):
                        c = half * 4 + cc
                        w = []
                        if cc == 0:
                            w = [(sm["xn"], sm["xn"].n),
                                 (sm["xnT"], max(0, sm["xnT"].n - 1 + half)),
                                 (sm["xnT32"], max(0, sm["xnT32"].n - 1 + half))]
                        _op(P, "tensor", lambda e, t=t, c=c, trp=trp, cc=cc:
                            e.transpose(
                                trp[:, cc * 128:(cc + 1) * 128],
                                xn32_rot[t % 2][:, c * 128:(c + 1) * 128], ident32),
                            waits=w, inc=(sm["tr"] if cc == 3 else None))
                for half, trp in ((0, tra), (1, trb)):
                    ew = [(sm["tr"], sm["tr"].n - 1 + half),
                          (sm["st_xnT"], 16 * s),
                          (sm["prj"], sm["prj"].n)]
                    if t == 0 and half == 0:
                        ew.append((sm["h2mm"], sm["h2mm"].n))
                    _op(P, "scalar", lambda e, t=t, half=half, trp=trp: e.copy(
                        out=xnT_sb.rearrange("p (c t2) -> p c t2", t2=T)
                            [:, half * 4:(half + 1) * 4, t * 128:(t + 1) * 128],
                        in_=trp.rearrange("p (c t2) -> p c t2", t2=128)),
                        waits=ew, inc=sm["xnT"])
                    _op(P, "scalar", lambda e, t=t, half=half, trp=trp: e.copy(
                        out=xnT32_rot[t % 2][:, half * 512:(half + 1) * 512],
                        in_=trp),
                        inc=sm["xnT32"])
                for c in range(8):
                    w = []
                    if c == 0:
                        w = [(sm["xnT32"], sm["xnT32"].n), (sm["ld_misc"], 48),
                             (sm["gv1"], sm["gv1"].n)]
                    _op(P, "tensor", lambda e, t=t, c=c: e.matmul(
                        psr[:, 0:8], xnT32_rot[t % 2][:, c * 128:(c + 1) * 128],
                        router_sb[:, c * 8:(c + 1) * 8],
                        start=(c == 0), stop=(c == 7)),
                        waits=w, inc=(sm["rmm"] if c == 7 else None))
                _op(P, "vector", lambda e: e.reduce_max(
                    out=gsc[:, 0:1], in_=psr[:, 0:8], axis=AX.X),
                    waits=[(sm["rmm"], sm["rmm"].n)], inc=sm["vch"])
                _op(P, "vector", lambda e: e.tensor_single_scalar(
                    out=gsc[:, 8:16], in_=psr[:, 0:8], scalar=gsc[:, 0:1],
                    op=ALU.subtract),
                    waits=[(sm["vch"], sm["vch"].n)],
                    inc=sm["gv1"])
                _op(P, "scalar", lambda e: e.activation(
                    out=gsc[:, 16:24], in_=gsc[:, 8:16], func=AF.Exp),
                    waits=[(sm["gv1"], sm["gv1"].n)], inc=sm["gs1"])
                _op(P, "vector", lambda e: e.tensor_single_scalar(
                    out=gsc[:, 24:32], in_=gsc[:, 8:16], scalar=0.0, op=ALU.is_ge),
                    waits=[(sm["gv1"], sm["gv1"].n)], inc=sm["vch"])
                _op(P, "vector", lambda e: e.scalar_tensor_tensor(
                    out=gsc[:, 32:40], in0=gsc[:, 24:32], scalar=-1e9,
                    in1=gsc[:, 8:16], op0=ALU.mult, op1=ALU.add),
                    waits=[(sm["vch"], sm["vch"].n)], inc=sm["vch"])
                _op(P, "vector", lambda e: e.reduce_max(
                    out=gsc[:, 1:2], in_=gsc[:, 32:40], axis=AX.X),
                    waits=[(sm["vch"], sm["vch"].n)], inc=sm["vch"])
                _op(P, "vector", lambda e: e.tensor_single_scalar(
                    out=gsc[:, 24:32], in_=gsc[:, 8:16], scalar=gsc[:, 1:2],
                    op=ALU.is_ge),
                    waits=[(sm["vch"], sm["vch"].n)], inc=sm["vch"])
                _op(P, "vector", lambda e: e.tensor_tensor(
                    out=gsc[:, 32:40], in0=gsc[:, 16:24], in1=gsc[:, 24:32],
                    op=ALU.mult),
                    waits=[(sm["gs1"], sm["gs1"].n), (sm["vch"], sm["vch"].n)],
                    inc=sm["vch"])
                _op(P, "vector", lambda e: e.reduce_sum(
                    out=gsc[:, 2:3], in_=gsc[:, 32:40], axis=AX.X),
                    waits=[(sm["vch"], sm["vch"].n)], inc=sm["vch"])
                _op(P, "vector", lambda e: e.reciprocal(
                    out=gsc[:, 3:4], in_=gsc[:, 2:3]),
                    waits=[(sm["vch"], sm["vch"].n)], inc=sm["vch"])
                _op(P, "vector", lambda e, t=t: e.tensor_single_scalar(
                    out=g_tok[:, t * 8:(t + 1) * 8], in_=gsc[:, 32:40],
                    scalar=gsc[:, 3:4], op=ALU.mult),
                    waits=[(sm["vch"], sm["vch"].n)],
                    inc=sm["gv2"])
                _op(P, "scalar", lambda e, t=t: e.copy(
                    out=gbf, in_=g_tok[:, t * 8:(t + 1) * 8]),
                    waits=[(sm["gv2"], sm["gv2"].n)], inc=sm["gs1"])
                _op(P, "tensor", lambda e: e.transpose(
                    psg[0:8, 0:128], gbf, ident_bf),
                    waits=[(sm["gs1"], sm["gs1"].n), (sm["gexp"], sm["gexp"].n)],
                    inc=sm["gtr"])
                _op(P, "scalar", lambda e, t=t: e.copy(
                    out=g_exp[0:8, t * 128:(t + 1) * 128], in_=psg[0:8, 0:128]),
                    waits=[(sm["gtr"], sm["gtr"].n), (sm["h2mm"], sm["h2mm"].n)],
                    inc=sm["gexp"])

            _op(P, "sync", lambda e: e.dma_start(
                out=s_xn[s].rearrange("c p t2 -> p c t2"),
                in_=xnT_sb.rearrange("p (c t2) -> p c t2", t2=T)),
                waits=[(sm["xnT"], sm["xnT"].n)], inc=sm["st_xnT"], inck=16)
            P.mark(f"LN{s}")

            ps_st = [P01, P23]
            ps_yt = P45
            ps_proj = P67[:, 0:512]
            ps_rlb = P67[:, 512:1024]

            _op(P, "sync", lambda e: e.dma_start(
                out=kvd_sb.rearrange("p (c j) -> p c j", j=128),
                in_=s_kvd.rearrange("(c p) j -> p c j", p=128)),
                waits=[(sm["st_wbf"], wbf_attn_done)], inc=sm["ld_attnw"], inck=16)
            _op(P, "sync", lambda e: e.dma_start(out=kvu_sb, in_=s_kvu),
                inc=sm["ld_attnw"], inck=16)
            _op(P, "sync", lambda e: e.dma_start(
                out=wq_sb.rearrange("p (c j) -> p c j", j=1024),
                in_=s_wq.rearrange("(c p) j -> p c j", p=128)),
                inc=sm["ld_attnw"], inck=16)
            _op(P, "sync", lambda e: e.dma_start(
                out=wo_sb.rearrange("p (h j) -> p h j", j=1024),
                in_=s_wo.rearrange("(h p) j -> p h j", p=128)),
                inc=sm["ld_attnw"], inck=16)
            attnw_done = 16 * 4 * (s + 1)
            drip(32)

            for g4 in range(4):
                for c in range(8):
                    w = []
                    if g4 == 0 and c == 0:
                        w = [(sm["ld_attnw"], attnw_done),
                             (sm["xnT"], sm["xnT"].n),
                             (sm["xnT32"], sm["xnT32"].n),
                             (sm["gv1"], sm["gv1"].n),
                             (sm["gexp"], sm["gexp"].n),
                             (sm["hev"], sm["hev"].n),
                             (sm["accv"], sm["accv"].n),
                             (sm["rl"], sm["rl"].n),
                             (sm["rlbe"], sm["rlbe"].n),
                             (sm["ynorm"], sm["ynorm"].n),
                             (sm["prjE"], sm["prjE"].n)]
                    elif c == 0:
                        w = [(sm["prjE"], sm["prjE"].n)]
                    _op(P, "tensor", lambda e, g4=g4, c=c: e.matmul(
                        ps_proj, kvd_sb[:, c * 128:(c + 1) * 128],
                        xnT_sb[:, c * T + g4 * 512: c * T + (g4 + 1) * 512],
                        start=(c == 0), stop=(c == 7)),
                        waits=w, inc=(sm["prj"] if c == 7 else None))
                lw = [(sm["prj"], sm["prj"].n)]
                if s > 0 and g4 == 0:
                    lw.append((sm["h2mm"], sm["h2mm"].n))
                _op(P, "scalar", lambda e, g4=g4: e.copy(
                    out=latentT[:, g4 * 512:(g4 + 1) * 512], in_=ps_proj),
                    waits=lw, inc=sm["prjE"])

            pending_rl = [None]

            def _emit_pv(p, hh, kc, g, expval, ynf, last=False):
                srot = kc % 2
                d = max(0, 128 * kc - 1024 * g)
                if d < 512:
                    wins = [(d, 512), (512, 1024)]
                else:
                    wins = [(d, 1024)]
                for wi, (a, b) in enumerate(wins):
                    w = [(sm["expo"], expval)]
                    if kc == 0 and wi == 0 and ynf:
                        w.append((sm["ynorm"], ynf))
                    lastwin = last and wi == len(wins) - 1
                    _op(P, "tensor",
                        lambda e, p=p, hh=hh, kc=kc, srot=srot, a=a, b=b,
                        st=(kc == 0), sp=lastwin: e.matmul(
                            ps_yt[0:65, a:b],
                            v_rot[p % 2][:, kc * 130 + hh * 65:
                                         kc * 130 + hh * 65 + 65],
                            pt_rot[srot][:, a:b],
                            start=st, stop=sp, skip_group_check=True),
                        waits=w,
                        inc=(sm["pv"] if wi == len(wins) - 1 else None))

            def _mk_rl(p, g, hh, bh):
                steps = []
                for half in range(2):
                    def s_recip(half=half):
                        def mk(e):
                            with nc.allow_low_precision(reason="bf16 softmax norm"):
                                return e.reciprocal(
                                    out=rl_bf[64:65, 0:512],
                                    in_=ps_yt[64:65, half * 512:(half + 1) * 512])
                        _op(P, "vector", mk,
                            waits=[(sm["pv"], sm["pv"].n),
                                   (sm["rlb"], sm["rlb"].n)],
                            inc=sm["rl"])
                    def s_bcast(half=half):
                        _op(P, "tensor", lambda e: e.matmul(
                            ps_rlb[0:64, 0:512], ones_bf[64:65, 0:64],
                            rl_bf[64:65, 0:512], start=True, stop=True),
                            waits=[(sm["rl"], sm["rl"].n),
                                   (sm["rlbe"], sm["rlbe"].n)],
                            inc=sm["rlb"])
                    def s_copy(half=half):
                        _op(P, "scalar", lambda e: e.copy(
                            out=rlb32, in_=ps_rlb[0:64, 0:512]),
                            waits=[(sm["rlb"], sm["rlb"].n),
                                   (sm["ynorm"], sm["ynorm"].n)],
                            inc=sm["rlbe"])
                    def s_ynorm(half=half):
                        _op(P, "vector",
                            lambda e, p=p, g=g, half=half, bh=bh:
                            e.tensor_tensor(
                                out=yT_all[bh:bh + 64,
                                           p * T + g * 1024 + half * 512:
                                           p * T + g * 1024 + (half + 1) * 512],
                                in0=ps_yt[0:64, half * 512:(half + 1) * 512],
                                in1=rlb32, op=ALU.mult),
                            waits=[(sm["rlbe"], sm["rlbe"].n)],
                            inc=sm["ynorm"])
                    steps += [s_recip, s_bcast, s_copy, s_ynorm]
                return steps

            for p in range(8):
                for g4 in range(4):
                    _op(P, "tensor", lambda e, p=p, g4=g4: e.matmul(
                        ps_proj, kvu_sb[:, p * 128:(p + 1) * 128],
                        latentT[:, g4 * 512:(g4 + 1) * 512],
                        start=True, stop=True),
                        waits=[(sm["prjE"], sm["prjE"].n)],
                        inc=sm["prj"])
                    _op(P, "scalar", lambda e, p=p, g4=g4: e.copy(
                        out=kT_rot[p % 2][:, g4 * 512:(g4 + 1) * 512],
                        in_=ps_proj),
                        waits=[(sm["prj"], sm["prj"].n)], inc=sm["prjE"])
                for g4 in range(4):
                    for c in range(8):
                        w = [(sm["prjE"], sm["prjE"].n)] if c == 0 else []
                        _op(P, "tensor", lambda e, p=p, g4=g4, c=c: e.matmul(
                            ps_proj,
                            wq_sb[:, c * 1024 + p * 128: c * 1024 + (p + 1) * 128],
                            xnT_sb[:, c * T + g4 * 512: c * T + (g4 + 1) * 512],
                            start=(c == 0), stop=(c == 7)),
                            waits=w, inc=(sm["prj"] if c == 7 else None))
                    _op(P, "scalar", lambda e, p=p, g4=g4: e.mul(
                        out=qT_rot[p % 2][:, g4 * 512:(g4 + 1) * 512],
                        in_=ps_proj, mul=0.125),
                        waits=[(sm["prj"], sm["prj"].n)], inc=sm["prjE"])
                mw = [(sm["pv"], sm["pv"].n), (sm["st_out"], sm["st_out"].n),
                      (sm["h2mm"], sm["h2mm"].n)]
                _op(P, "vector", lambda e, p=p: e.memset(
                    v_rot[p % 2].rearrange("p2 (k h j) -> p2 k h j", h=2, j=65)
                    [:, :, :, 64:65], 1.0),
                    waits=mw, inc=sm["vmem"])
                vmset = sm["vmem"].n
                for q4 in range(4):
                    for tq in range(4):
                        t = q4 * 4 + tq
                        w = [(sm["prjE"], sm["prjE"].n)] if tq == 0 else []
                        _op(P, "tensor", lambda e, p=p, t=t, tq=tq: e.matmul(
                            ps_proj[:, tq * 128:(tq + 1) * 128],
                            latentT[:, t * 128:(t + 1) * 128],
                            kvu_sb[:, 1024 + p * 128:1024 + (p + 1) * 128],
                            start=True, stop=True),
                            waits=w, inc=(sm["prj"] if tq == 3 else None))
                    _op(P, "vector", lambda e, p=p, q4=q4: e.tensor_copy(
                        out=v_rot[p % 2]
                            .rearrange("p2 (k h j) -> p2 k h j", h=2, j=65)
                            [:, q4 * 4:(q4 + 1) * 4, :, 0:64],
                        in_=ps_proj.rearrange("p2 (k h j) -> p2 k h j", h=2, j=64)),
                        waits=[(sm["prj"], sm["prj"].n),
                               (sm["cast"], vmset),
                               (sm["pv"], sm["pv"].n)],
                        inc=sm["prjE"])
                for hh in range(2):
                    bh = 64 * hh
                    for g in range(2):
                        nkc = 8 * (g + 1)
                        ynorm_final = sm["ynorm"].n + (2 if pending_rl[0] else 0)
                        expv = {}
                        for kc in range(nkc):
                            srot = kc % 2
                            d = max(0, 128 * kc - 1024 * g)
                            if d < 512:
                                wins = [(d, 512), (512, 1024)]
                            else:
                                wins = [(d, 1024)]
                            diag = 128 * kc >= 1024 * g
                            for wi, (a, b) in enumerate(wins):
                                w = []
                                if wi == 0:
                                    w = [(sm["prjE"], sm["prjE"].n),
                                         (sm["expo"], max(0, sm["expo"].n - 1))]
                                _op(P, "tensor",
                                    lambda e, p=p, kc=kc, g=g, a=a, b=b,
                                    srot=srot, bh=bh: e.matmul(
                                        ps_st[srot][:, a:b],
                                        kT_rot[p % 2][bh:bh + 64,
                                                      kc * 128:(kc + 1) * 128],
                                        qT_rot[p % 2][bh:bh + 64,
                                                      g * 1024 + a: g * 1024 + b],
                                        start=True, stop=True),
                                    waits=w,
                                    inc=(sm["qk"] if wi == len(wins) - 1 else None))
                            if diag:
                                _op(P, "vector", lambda e, srot=srot, d=d:
                                    e.tensor_tensor(
                                        out=ps_st[srot][:, d:d + 128],
                                        in0=ps_st[srot][:, d:d + 128],
                                        in1=mask_sb, op=ALU.add),
                                    waits=[(sm["qk"], sm["qk"].n)], inc=sm["msk"])
                            ew = [(sm["qk"], sm["qk"].n),
                                  (sm["pv"], sm["pv"].n)]
                            if diag:
                                ew.append((sm["msk"], sm["msk"].n))
                            _op(P, "scalar", lambda e, srot=srot, d=d: e.activation(
                                out=pt_rot[srot][:, d:1024],
                                in_=ps_st[srot][:, d:1024], func=AF.Exp),
                                waits=ew, inc=sm["expo"])
                            expv[kc] = sm["expo"].n
                            if kc <= 1 and pending_rl[0]:
                                steps = pending_rl[0][kc * 4:(kc + 1) * 4]
                                for st_fn in steps:
                                    st_fn()
                                if kc == 1:
                                    pending_rl[0] = None
                            if kc >= 1:
                                _emit_pv(p, hh, kc - 1, g, expv[kc - 1],
                                         ynorm_final if kc - 1 == 0 else None)
                        _emit_pv(p, hh, nkc - 1, g, expv[nkc - 1], None, last=True)
                        pending_rl[0] = _mk_rl(p, g, hh, bh)
            if pending_rl[0]:
                for st_fn in pending_rl[0]:
                    st_fn()
                pending_rl[0] = None

            ps_at = [P01, P23]
            for t in range(NTC):
                ldx_ct[0] += 1
                myx = 16 * ldx_ct[0]
                _op(P, "sync", lambda e, t=t: e.dma_start(
                    out=x_rot[t % 2], in_=d_x[s, t * 128:(t + 1) * 128, :]),
                    waits=[(sm["st_acc"], 16 * max(0, sm["st_acc"].n // 16 - 1)),
                           (sm["xn"], sm["xn"].n)],
                    inc=sm["ld_x"], inck=16)
                for pair in range(8):
                    for half in range(2):
                        w = []
                        if pair == 0 and half == 0:
                            w = [(sm["ynorm"], sm["ynorm"].n),
                                 (sm["expo"], sm["expo"].n),
                                 (sm["msk"], sm["msk"].n),
                                 (sm["gexp"], sm["gexp"].n),
                                 (sm["gv1"], sm["gv1"].n),
                                 (sm["accv"], max(0, sm["accv"].n - 1))]
                        _op(P, "tensor",
                            lambda e, t=t, pair=pair, half=half: e.matmul(
                                ps_at[t % 2][:, half * 512:(half + 1) * 512],
                                yT_all[:, pair * T + t * 128: pair * T + (t + 1) * 128],
                                wo_sb[:, pair * 1024 + half * 512:
                                      pair * 1024 + (half + 1) * 512],
                                start=(pair == 0), stop=(pair == 7)),
                            waits=w,
                            inc=(sm["wo"] if pair == 7 and half == 1 else None))
                _op(P, "vector", lambda e, t=t: e.scalar_tensor_tensor(
                    out=x_rot[t % 2], in0=ps_at[t % 2], scalar=1.0,
                    in1=x_rot[t % 2], op0=ALU.mult, op1=ALU.add),
                    waits=[(sm["wo"], sm["wo"].n), (sm["ld_x"], myx)],
                    inc=sm["accv"])
                _op(P, "sync", lambda e, t=t: e.dma_start(
                    out=s_acc[s, t * 128:(t + 1) * 128, :], in_=x_rot[t % 2]),
                    waits=[(sm["accv"], sm["accv"].n)], inc=sm["st_acc"], inck=16)
                drip(24)
            if debug:
                _op(P, "sync", lambda e: e.dma_start(
                    out=d_gtok[s], in_=g_tok),
                    waits=[(sm["gv2"], sm["gv2"].n)], inc=sm["st_acc"], inck=16)

        def seq_phase_B(s):
            ps_h1 = [P01, P23]
            ps_h2 = [P45, P67]
            for r in range(NSTR):
                Rg = s * NSTR + r
                _op(P, "sync", lambda e, r=r: e.dma_start(
                    out=xn_str.rearrange("p (c t2) -> p c t2", t2=STRIPE),
                    in_=s_xn[s][:, :, r * STRIPE:(r + 1) * STRIPE]
                        .rearrange("c p t2 -> p c t2")),
                    waits=[(sm["st_xnT"], 16 * (s + 1)),
                           (sm["h1f"], sm["h1f"].n),
                           (sm["st_acc"], 16 * 16 * (s + 1)),
                           (sm["accv"], 16 * (s + 1)),
                           (sm["wo"], 16 * (s + 1))],
                    inc=sm["ld_xn"], inck=16)
                _op(P, "sync", lambda e, r=r: e.dma_start(
                    out=acc_sb.rearrange("p (tc c2) -> p tc c2", c2=1024),
                    in_=s_acc[s, r * STRIPE:(r + 1) * STRIPE, :]
                        .rearrange("(tc p) c2 -> p tc c2", p=128)),
                    waits=[(sm["st_out"], 16 * Rg),
                           (sm["rlbe"], sm["rlbe"].n)],
                    inc=sm["ld_acc"], inck=16)
                for tc in range(8):
                    for half in range(2):
                        w = []
                        if half == 0:
                            w = [(sm["gexp"], 16 * (s + 1)),
                                 (sm["cast"], CAST0),
                                 (sm["hev"], max(0, sm["hev"].n - 1))]
                            if tc == 0:
                                w += [(sm["ynorm"], sm["ynorm"].n),
                                      (sm["rl"], sm["rl"].n),
                                      (sm["rlbe"], sm["rlbe"].n),
                                      (sm["prjE"], sm["prjE"].n)]
                        _op(P, "tensor", lambda e, r=r, tc=tc, half=half: e.matmul(
                            ps_h2[tc % 2][:, half * 512:(half + 1) * 512],
                            g_exp[0:8, (r * 8 + tc) * 128:(r * 8 + tc + 1) * 128],
                            b2_bf[0:8, half * 512:(half + 1) * 512],
                            start=True, stop=True),
                            waits=w, inc=(sm["h2mm"] if half == 1 else None))
                    _op(P, "vector", lambda e, tc=tc, Rg=Rg: e.scalar_tensor_tensor(
                        out=acc_sb[:, tc * 1024:(tc + 1) * 1024],
                        in0=ps_h2[tc % 2], scalar=1.0,
                        in1=acc_sb[:, tc * 1024:(tc + 1) * 1024],
                        op0=ALU.mult, op1=ALU.add),
                        waits=[(sm["h2mm"], sm["h2mm"].n),
                               (sm["ld_acc"], 16 * (Rg + 1))],
                        inc=sm["hev"])
                for ex in range(E):
                    _op(P, "sync", lambda e, ex=ex: e.dma_start(
                        out=b1_cur[:, (ex % 2) * 32:(ex % 2) * 32 + 32],
                        in_=d_b1[ex]),
                        waits=[(sm["gelu"], max(0, sm["gelu"].n - 32))],
                        inc=sm["ld_b1"], inck=16)
                    drip(192)
                    b1done = 16 * (Rg * 8 + ex + 1)
                    w2done = None
                    for f in range(32):
                        slot = glob_slab[0] % 8
                        glob_slab[0] += 1
                        myslab = glob_slab[0]
                        _op(P, "sync", lambda e, ex=ex, f=f, slot=slot: e.dma_start(
                            out=w1_ring[:, slot * 1024:(slot + 1) * 1024]
                                .rearrange("p (k j) -> p k j", j=128),
                            in_=s_w1[ex][:, f * 128:(f + 1) * 128]
                                .rearrange("(k p) j -> p k j", p=128)),
                            waits=[(sm["st_wbf"], wbf_w1_done[ex]),
                                   (sm["h1f"], max(0, myslab - 8))],
                            inc=sm["ld_w1"], inck=16)
                        if f == 7:
                            for q in range(4):
                                _op(P, "sync", lambda e, ex=ex, q=q: e.dma_start(
                                    out=w2_sb.rearrange("p (f2 j) -> p f2 j", j=1024)
                                        [:, q * 8:(q + 1) * 8, :],
                                    in_=s_w2[ex].rearrange("(f2 p) j -> p f2 j", p=128)
                                        [:, q * 8:(q + 1) * 8, :]),
                                    waits=[(sm["st_wbf"], wbf_w2_done[ex]),
                                           (sm["h2mm"], sm["h2mm"].n)],
                                    inc=sm["ld_w2"], inck=16)
                            w2done = 16 * ((Rg * 8 + ex + 1) * 4)
                        for g2 in range(2):
                            for k in range(8):
                                w = []
                                if g2 == 0 and k == 0:
                                    w = [(sm["ld_w1"], 16 * myslab),
                                         (sm["ld_xn"], 16 * (Rg + 1)),
                                         (sm["gelu"], max(0, sm["gelu"].n - 1))]
                                    if f == 0 and ex == 0:
                                        w.append((sm["accv"], sm["accv"].n))
                                _op(P, "tensor",
                                    lambda e, f=f, g2=g2, k=k, slot=slot: e.matmul(
                                        ps_h1[f % 2][:, g2 * 512:(g2 + 1) * 512],
                                        w1_ring[:, slot * 1024 + k * 128:
                                                slot * 1024 + (k + 1) * 128],
                                        xn_str[:, k * STRIPE + g2 * 512:
                                               k * STRIPE + (g2 + 1) * 512],
                                        start=(k == 0), stop=(k == 7)),
                                    waits=w,
                                    inc=(sm["h1f"] if g2 == 1 and k == 7 else None))
                        _op(P, "scalar", lambda e, ex=ex, f=f: e.activation(
                            out=h1T[:, f * STRIPE:(f + 1) * STRIPE],
                            in_=ps_h1[f % 2], func=AF.Gelu,
                            bias=b1_cur[:, (ex % 2) * 32 + f:
                                        (ex % 2) * 32 + f + 1]),
                            waits=[(sm["h1f"], sm["h1f"].n),
                                   (sm["ld_b1"], b1done)],
                            inc=sm["gelu"])
                    geludone = sm["gelu"].n
                    for tc in range(8):
                        for f in range(32):
                            for half in range(2):
                                w = []
                                if f == 0 and half == 0:
                                    w = [(sm["gelu"], geludone),
                                         (sm["ld_w2"], w2done),
                                         (sm["hev"], max(0, sm["hev"].n - 1))]
                                _op(P, "tensor",
                                    lambda e, tc=tc, f=f, half=half: e.matmul(
                                        ps_h2[tc % 2][:, half * 512:(half + 1) * 512],
                                        h1T[:, f * STRIPE + tc * 128:
                                            f * STRIPE + (tc + 1) * 128],
                                        w2_sb[:, f * 1024 + half * 512:
                                              f * 1024 + (half + 1) * 512],
                                        start=(f == 0), stop=(f == 31)),
                                    waits=w,
                                    inc=(sm["h2mm"] if f == 31 and half == 1
                                         else None))
                        _op(P, "vector",
                            lambda e, r=r, tc=tc, ex=ex: e.scalar_tensor_tensor(
                                out=acc_sb[:, tc * 1024:(tc + 1) * 1024],
                                in0=ps_h2[tc % 2],
                                scalar=g_tok[:, (r * 8 + tc) * 8 + ex:
                                             (r * 8 + tc) * 8 + ex + 1],
                                in1=acc_sb[:, tc * 1024:(tc + 1) * 1024],
                                op0=ALU.mult, op1=ALU.add),
                            waits=[(sm["h2mm"], sm["h2mm"].n)],
                            inc=sm["hev"])
                _op(P, "sync", lambda e, s2=s, r=r: e.dma_start(
                    out=d_out[s2, r * STRIPE:(r + 1) * STRIPE, :]
                        .rearrange("(tc p) c2 -> p tc c2", p=128),
                    in_=acc_sb.rearrange("p (tc c2) -> p tc c2", c2=1024)),
                    waits=[(sm["hev"], sm["hev"].n)], inc=sm["st_out"], inck=16)

        P.mark("prepass")
        for s in range(NSEQ):
            seq_phase_A(s)
            P.mark(f"A{s}")
            seq_phase_B(s)
            P.mark(f"B{s}")

        drip(1 << 30)
        _op(P, "gpsimd", lambda e: e.wait_ge(sm["st_out"].h, 16 * NSEQ * NSTR))
        _op(P, "sync", lambda e: e.wait_ge(sm["st_out"].h, 16 * NSEQ * NSTR))

        if upto and str(upto).startswith("@"):
            cut = int(str(upto)[1:])
        else:
            cut = P.marks.get(upto, None) if upto else None
        build_info["marks"] = dict(P.marks)
        build_info["labels"] = list(P.labels)

        def _run(eng, key):
            for i, f in P.streams[key]:
                if cut is None or i <= cut:
                    f(eng)

        with nc.Block() as block:
            @block.sync
            def _(eng):
                _run(eng, "sync")

            @block.tensor
            def _(eng):
                _run(eng, "tensor")

            @block.vector
            def _(eng):
                _run(eng, "vector")

            @block.scalar
            def _(eng):
                _run(eng, "scalar")

            @block.gpsimd
            def _(eng):
                _run(eng, "gpsimd")

    return nc


def _host_inputs(L0_attn_x, L0_attn_w_kv_down, L0_attn_w_kv_up, L0_attn_w_q,
                 L0_attn_w_out, L0_ffn_router_w, L0_ffn_w1, L0_ffn_b1,
                 L0_ffn_w2, L0_ffn_b2):
    x = np.ascontiguousarray(np.asarray(L0_attn_x, dtype=np.float32))
    w_kv_down = np.ascontiguousarray(np.asarray(L0_attn_w_kv_down, dtype=np.float32))
    w_kv_up = np.ascontiguousarray(np.asarray(L0_attn_w_kv_up, dtype=np.float32))
    w_q = np.ascontiguousarray(np.asarray(L0_attn_w_q, dtype=np.float32))
    w_out = np.ascontiguousarray(np.asarray(L0_attn_w_out, dtype=np.float32))
    router_wt = np.ascontiguousarray(np.asarray(L0_ffn_router_w, dtype=np.float32).T)
    w1t = np.ascontiguousarray(np.asarray(L0_ffn_w1, dtype=np.float32).transpose(0, 2, 1))
    w2t = np.ascontiguousarray(np.asarray(L0_ffn_w2, dtype=np.float32).transpose(0, 2, 1))
    b1t = np.ascontiguousarray(
        np.asarray(L0_ffn_b1, dtype=np.float32).reshape(E, 32, 128).transpose(0, 2, 1))
    b2 = np.ascontiguousarray(np.asarray(L0_ffn_b2, dtype=np.float32))
    ident = np.eye(128, dtype=np.float32)
    kk, qq = np.meshgrid(np.arange(128), np.arange(128), indexing="ij")
    mask = np.where(qq >= kk, 0.0, -30000.0).astype(np.float32)
    shared = dict(w_kv_down=w_kv_down, w_kv_up=w_kv_up, w_q=w_q, w_out=w_out,
                  router_wt=router_wt, w1t=w1t, w2t=w2t, b1t=b1t, b2=b2,
                  ident=ident, mask_tri=mask)
    in_maps = []
    for c in range(8):
        m = dict(shared)
        m["x"] = np.ascontiguousarray(x[c * NSEQ:(c + 1) * NSEQ])
        in_maps.append(m)
    return in_maps


def kernel(**inputs):
    in_maps = _host_inputs(**inputs)
    nc = build_program(debug=DEBUG)
    res = run_bass_kernel_spmd(nc, in_maps, core_ids=list(range(8)))
    out = np.concatenate([r["out"] for r in res.results], axis=0)
    return out.astype(np.float32)


# version 07
# speedup vs baseline: 1.1390x; 1.1390x over previous
import numpy as np
import concourse.bass as bass
import concourse.mybir as mybir
from concourse.bass_utils import run_bass_kernel_spmd
from contextlib import ExitStack

F32 = mybir.dt.float32
BF16 = mybir.dt.bfloat16
AF = mybir.ActivationFunctionType
ALU = mybir.AluOpType
AX = mybir.AxisListType

NSEQ = 2
T = 2048
C = 1024
R = 128
E = 8
FF = 4096
NTC = T // 128
STRIPE = 1024
NSTR = T // STRIPE
EPS = 1e-5

DEBUG = False


class Sem:
    def __init__(self, h):
        self.h = h
        self.n = 0

    def bump(self, k=1):
        self.n += k
        return self.n


class Prog:
    def __init__(self, nc, es):
        self.nc = nc
        self.es = es
        self.streams = {k: [] for k in ("tensor", "vector", "scalar", "sync", "gpsimd")}
        self.ctr = 0
        self.marks = {}
        self.labels = []

    def sem(self, name):
        return Sem(self.es.enter_context(self.nc.semaphore(name)))

    def emit(self, eng, fn):
        self.ctr += 1
        self.streams[eng].append((self.ctr, fn))

    def mark(self, name):
        self.marks[name] = self.ctr


import inspect


def _op(P, eng, make, waits=(), inc=None, inck=1):
    P.labels.append((P.ctr + 1, eng, inspect.stack(0)[1].lineno))
    incs = []
    if inc is not None:
        if not isinstance(inc, list):
            inc = [(inc, inck)]
        for sem_, k_ in inc:
            sem_.bump(k_)
            incs.append((sem_.h, k_))
    wl = [(s.h, v) for (s, v) in waits if v > 0]

    def f(eng_obj):
        for h, v in wl:
            eng_obj.wait_ge(h, v)
        ins = make(eng_obj)
        for h, k_ in incs:
            ins.then_inc(h, k_)
    P.emit(eng, f)


build_info = {}


def build_program(debug=False, upto=None):
    nc = bass.Bass("TRN2", target_bir_lowering=False)

    d_x = nc.dram_tensor("x", [NSEQ, T, C], F32, kind="ExternalInput").ap()
    d_kvd = nc.dram_tensor("w_kv_down", [C, R], F32, kind="ExternalInput").ap()
    d_kvu = nc.dram_tensor("w_kv_up", [R, 2048], F32, kind="ExternalInput").ap()
    d_wq = nc.dram_tensor("w_q", [C, C], F32, kind="ExternalInput").ap()
    d_wo = nc.dram_tensor("w_out", [C, C], F32, kind="ExternalInput").ap()
    d_rw = nc.dram_tensor("router_wt", [C, E], F32, kind="ExternalInput").ap()
    d_w1 = nc.dram_tensor("w1t", [E, C, FF], F32, kind="ExternalInput").ap()
    d_w2 = nc.dram_tensor("w2t", [E, FF, C], F32, kind="ExternalInput").ap()
    d_b1 = nc.dram_tensor("b1t", [E, 128, 32], F32, kind="ExternalInput").ap()
    d_b2 = nc.dram_tensor("b2", [E, C], F32, kind="ExternalInput").ap()
    d_id = nc.dram_tensor("ident", [128, 128], F32, kind="ExternalInput").ap()
    d_mask = nc.dram_tensor("mask_tri", [128, 128], F32, kind="ExternalInput").ap()
    d_out = nc.dram_tensor("out", [NSEQ, T, C], F32, kind="ExternalOutput").ap()

    kscr = "ExternalOutput" if debug else "Internal"
    s_kvd = nc.dram_tensor("s_kvd", [C, R], BF16, kind="Internal").ap()
    s_kvu = nc.dram_tensor("s_kvu", [R, 2048], BF16, kind="Internal").ap()
    s_wq = nc.dram_tensor("s_wq", [C, C], BF16, kind="Internal").ap()
    s_wo = nc.dram_tensor("s_wo", [C, C], BF16, kind="Internal").ap()
    s_w1 = nc.dram_tensor("s_w1", [E, C, FF], BF16, kind="Internal").ap()
    s_w2 = nc.dram_tensor("s_w2", [E, FF, C], BF16, kind="Internal").ap()
    s_xn = nc.dram_tensor("s_xn", [NSEQ, 8, 128, T], BF16, kind=kscr).ap()
    s_acc = nc.dram_tensor("s_acc", [NSEQ, T, C], F32, kind=kscr).ap()
    if debug:
        d_gtok = nc.dram_tensor("g_tok_dbg", [NSEQ, 128, NTC * 8], F32,
                                kind="ExternalOutput").ap()

    SB_BASE = 16512
    SB_TOP = 229376
    off = [SB_BASE]

    def at(name, shape, dtype, o=None):
        nbytes = int(np.prod(shape[1:])) * (2 if dtype == BF16 else 4)
        if o is None:
            o = off[0]
            off[0] = (o + nbytes + 31) & ~31
        return nc.alloc_sbuf_tensor_at(name, shape, dtype, offset=o).ap()

    ident32 = at("ident32", [128, 128], F32)
    ident_bf = at("ident_bf", [128, 128], BF16)
    mask_sb = at("mask_sb", [128, 128], F32)
    ones_bf = at("ones_bf", [128, 64], BF16)
    router_sb = at("router_sb", [128, 64], F32)
    g_tok = at("g_tok", [128, NTC * 8], F32)
    g_exp = at("g_exp", [8, T], BF16)
    b2_bf = at("b2_bf", [8, C], BF16)
    b1_cur = at("b1_cur", [128, 64], F32)
    stats = at("stats", [128, 16], F32)
    gsc = at("gsc", [128, 48], F32)
    gbf = at("gbf", [128, 8], BF16)
    RB = (off[0] + 31) & ~31

    off[0] = RB
    x_rot = [at(f"x_rot{i}", [128, 1024], F32) for i in range(2)]
    xn32_rot = [at(f"xn32_{i}", [128, 1024], F32) for i in range(2)]
    xnT32_rot = [at(f"xnT32_{i}", [128, 1024], F32) for i in range(2)]
    junk_sq = at("junk_sq", [128, 1024], F32)
    xnT_base = off[0]
    xnT_sb = at("xnT_sb", [128, 8 * T], BF16)
    yT_all = at("yT_all", [128, 8 * T], BF16)
    latentT = at("latentT", [128, T], BF16)
    kT_rot = [at(f"kT_{i}", [128, T], BF16) for i in range(2)]
    qT_rot = [at(f"qT_{i}", [128, T], BF16) for i in range(2)]
    v_rot = [at(f"v_{i}", [128, NTC * 130], BF16) for i in range(2)]
    pt_rot = [at(f"pt_{i}", [128, 1024], BF16) for i in range(2)]
    kvd_sb = at("kvd_sb", [128, 1024], BF16)
    kvu_sb = at("kvu_sb", [128, 2048], BF16)
    wq_sb = at("wq_sb", [128, 8192], BF16)
    wo_sb = at("wo_sb", [128, 8192], BF16)
    rl_bf = at("rl_bf", [128, 512], BF16)
    rlb32 = at("rlb32", [64, 512], F32)
    a_end = off[0]
    off[0] = xnT_base
    b2_32 = at("b2_32", [8, C], F32)
    off[0] = RB
    w2_sb = at("w2_sb", [128, 32 * 1024], BF16)
    w1_ring = at("w1_ring", [128, 4 * 1024], BF16)
    h1T = at("h1T", [128, 32 * STRIPE], BF16)
    acc_sb = at("acc_sb", [128, 8 * 1024], F32)
    xn_str = at("xn_str", [128, 8 * STRIPE], BF16)
    b_end = off[0]
    stage32 = [at(f"stage32_{i}", [128, 1024], F32, o=b_end + 4096 * i)
               for i in range(2)]
    stagebf = [at(f"stagebf_{i}", [128, 1024], BF16, o=b_end + 8192 + 2048 * i)
               for i in range(2)]
    assert b_end + 12288 <= SB_TOP and a_end <= SB_TOP, (a_end, b_end)

    es_top = ExitStack()
    with es_top:
        P = Prog(nc, es_top)
        P01 = es_top.enter_context(nc.psum_tensor("P01", [128, 1024], F32))[:]
        P23 = es_top.enter_context(nc.psum_tensor("P23", [128, 1024], F32))[:]
        P45 = es_top.enter_context(nc.psum_tensor("P45", [128, 1024], F32))[:]
        P67 = es_top.enter_context(nc.psum_tensor("P67", [128, 1024], F32))[:]

        sm = {name: P.sem(name) for name in [
            "ld_pre", "ld_misc", "ld_b2", "ld_x", "ld_attnw", "ld_xn", "ld_acc",
            "ld_w1", "ld_w2", "ld_b1",
            "st_wbf", "st_xnT", "st_acc", "st_out",
            "cast", "tr", "xnT", "xnT32", "rmm", "gv1", "gs1", "gv2", "gtr", "gexp",
            "sq", "vstat", "sstat", "xn",
            "prj", "prjE", "qk", "msk", "expo", "pv", "rl", "rlb", "rlbe", "ynorm",
            "wo", "accv",
            "h1f", "gelu", "h2mm", "hev", "vch", "vmem",
        ]}

        _op(P, "sync", lambda e: e.dma_start(out=ident32, in_=d_id),
            inc=sm["ld_misc"], inck=16)
        _op(P, "sync", lambda e: e.dma_start(out=mask_sb, in_=d_mask),
            inc=sm["ld_misc"], inck=16)
        _op(P, "sync", lambda e: e.dma_start(
            out=router_sb.rearrange("p (c e2) -> p c e2", e2=8),
            in_=d_rw.rearrange("(c p) e2 -> p c e2", p=128)),
            inc=sm["ld_misc"], inck=16)
        _op(P, "sync", lambda e: e.dma_start(out=b2_32, in_=d_b2),
            inc=sm["ld_b2"], inck=16)

        _op(P, "scalar", lambda e: e.copy(out=ident_bf, in_=ident32),
            waits=[(sm["ld_misc"], 48)], inc=sm["cast"])
        _op(P, "vector", lambda e: e.memset(ones_bf, 1.0), inc=sm["cast"])
        _op(P, "gpsimd", lambda e: e.tensor_copy(out=b2_bf, in_=b2_32),
            waits=[(sm["ld_b2"], 16)], inc=sm["cast"])
        CAST0 = 3

        cast_jobs = []
        def add_cast_jobs(src, dst, rows, cols):
            cw = min(1024, cols)
            for r0 in range(0, rows, 128):
                for c0 in range(0, cols, cw):
                    cast_jobs.append((src[r0:r0 + 128, c0:c0 + cw],
                                      dst[r0:r0 + 128, c0:c0 + cw], cw))

        add_cast_jobs(d_kvd, s_kvd, C, R)
        add_cast_jobs(d_kvu, s_kvu, R, 2048)
        add_cast_jobs(d_wq, s_wq, C, C)
        add_cast_jobs(d_wo, s_wo, C, C)
        n_attn_jobs = len(cast_jobs)
        w1_done_idx = []
        w2_done_idx = []
        for e_ in range(E):
            add_cast_jobs(d_w1[e_], s_w1[e_], C, FF)
            w1_done_idx.append(len(cast_jobs))
            add_cast_jobs(d_w2[e_], s_w2[e_], FF, C)
            w2_done_idx.append(len(cast_jobs))
        njobs = len(cast_jobs)

        drip_jobs = []
        for j, (srcap, dstap, cw) in enumerate(cast_jobs):
            def mk_load(j=j, srcap=srcap, cw=cw):
                _op(P, "sync", lambda e, s=srcap, b=stage32[j % 2], w=cw:
                    e.dma_start(out=b[:, 0:w], in_=s),
                    waits=[(sm["cast"], CAST0 + max(0, j - 1))],
                    inc=sm["ld_pre"], inck=16)
            drip_jobs.append(mk_load)
            _op(P, "gpsimd", lambda e, b32=stage32[j % 2], bbf=stagebf[j % 2], w=cw:
                e.tensor_copy(out=bbf[:, 0:w], in_=b32[:, 0:w]),
                waits=[(sm["ld_pre"], 16 * (j + 1)),
                       (sm["st_wbf"], 16 * max(0, j - 1))],
                inc=sm["cast"])
            _op(P, "gpsimd", lambda e, d=dstap, bbf=stagebf[j % 2], w=cw:
                e.dma_start(out=d, in_=bbf[:, 0:w]),
                waits=[(sm["cast"], CAST0 + j + 1)],
                inc=sm["st_wbf"], inck=16)
        drip_state = [0]

        def drip(n):
            a = drip_state[0]
            b = min(a + n, len(drip_jobs))
            for i in range(a, b):
                drip_jobs[i]()
            drip_state[0] = b
        wbf_attn_done = 16 * n_attn_jobs
        wbf_w1_done = [16 * i for i in w1_done_idx]
        wbf_w2_done = [16 * i for i in w2_done_idx]

        glob_slab = [0]
        ldx_ct = [0]

        def seq_phase_A(s):
            tra = P01[:, 0:512]
            trb = P01[:, 512:1024]
            psr = P23[:, 0:512]
            psg = P23[:, 512:1024].bitcast(BF16)

            for t in range(NTC):
                ldx_ct[0] += 1
                myx = 16 * ldx_ct[0]
                w = [(sm["xn"], max(0, sm["xn"].n - 1)),
                     (sm["sq"], max(0, sm["sq"].n - 1)),
                     (sm["st_acc"], 16 * 16 * s)]
                if t == 0:
                    w += [(sm["cast"], CAST0),
                          (sm["st_out"], 16 * NSTR * s),
                          (sm["h2mm"], sm["h2mm"].n)]
                _op(P, "sync", lambda e, t=t: e.dma_start(
                    out=x_rot[t % 2], in_=d_x[s, t * 128:(t + 1) * 128, :]),
                    waits=w, inc=sm["ld_x"], inck=16)
                drip(12)
                so = 8 * (t % 2)
                sqw = [(sm["ld_x"], myx), (sm["xn"], max(0, sm["xn"].n - 1))]
                if t == 0:
                    sqw.append((sm["h2mm"], sm["h2mm"].n))
                _op(P, "scalar", lambda e, t=t, so=so: e.activation(
                    out=junk_sq, in_=x_rot[t % 2], func=AF.Square,
                    accum_out=stats[:, so + 0:so + 1]),
                    waits=sqw, inc=sm["sq"])
                _op(P, "vector", lambda e, t=t, so=so: e.reduce_sum(
                    out=stats[:, so + 1:so + 2], in_=x_rot[t % 2], axis=AX.X),
                    waits=[(sm["ld_x"], myx)], inc=sm["vch"])
                _op(P, "vector", lambda e, so=so: e.tensor_scalar_mul(
                    out=stats[:, so + 2:so + 3], in0=stats[:, so + 1:so + 2],
                    scalar1=1.0 / C),
                    waits=[(sm["vch"], sm["vch"].n)], inc=sm["vch"])
                _op(P, "vector", lambda e, so=so: e.tensor_tensor(
                    out=stats[:, so + 3:so + 4], in0=stats[:, so + 2:so + 3],
                    in1=stats[:, so + 2:so + 3], op=ALU.mult),
                    waits=[(sm["vch"], sm["vch"].n)], inc=sm["vch"])
                _op(P, "vector", lambda e, so=so: e.tensor_scalar_mul(
                    out=stats[:, so + 4:so + 5], in0=stats[:, so + 0:so + 1],
                    scalar1=1.0 / C),
                    waits=[(sm["sq"], sm["sq"].n)], inc=sm["vch"])
                _op(P, "vector", lambda e, so=so: e.tensor_tensor(
                    out=stats[:, so + 5:so + 6], in0=stats[:, so + 4:so + 5],
                    in1=stats[:, so + 3:so + 4], op=ALU.subtract),
                    waits=[(sm["vch"], sm["vch"].n)], inc=sm["vch"])
                _op(P, "vector", lambda e, so=so: e.tensor_scalar_add(
                    out=stats[:, so + 6:so + 7], in0=stats[:, so + 5:so + 6],
                    scalar1=EPS),
                    waits=[(sm["vch"], sm["vch"].n)], inc=sm["vch"])
                _op(P, "vector", lambda e, so=so: e.reciprocal(
                    out=stats[:, so + 7:so + 8], in_=stats[:, so + 6:so + 7]),
                    waits=[(sm["vch"], sm["vch"].n)], inc=sm["vstat"])
                _op(P, "scalar", lambda e, so=so: e.activation(
                    out=stats[:, so + 6:so + 7], in_=stats[:, so + 7:so + 8],
                    func=AF.Sqrt),
                    waits=[(sm["vstat"], sm["vstat"].n)], inc=sm["sstat"])
                _op(P, "vector", lambda e, t=t, so=so: e.tensor_scalar(
                    out=xn32_rot[t % 2], in0=x_rot[t % 2],
                    scalar1=stats[:, so + 2:so + 3], scalar2=stats[:, so + 6:so + 7],
                    op0=ALU.subtract, op1=ALU.mult),
                    waits=[(sm["sstat"], sm["sstat"].n)],
                    inc=sm["xn"])
                for half, trp in ((0, tra), (1, trb)):
                    for cc in range(4):
                        c = half * 4 + cc
                        w = []
                        if cc == 0:
                            w = [(sm["xn"], sm["xn"].n),
                                 (sm["xnT"], max(0, sm["xnT"].n - 1 + half)),
                                 (sm["xnT32"], max(0, sm["xnT32"].n - 1 + half))]
                        _op(P, "tensor", lambda e, t=t, c=c, trp=trp, cc=cc:
                            e.transpose(
                                trp[:, cc * 128:(cc + 1) * 128],
                                xn32_rot[t % 2][:, c * 128:(c + 1) * 128], ident32),
                            waits=w, inc=(sm["tr"] if cc == 3 else None))
                for half, trp in ((0, tra), (1, trb)):
                    ew = [(sm["tr"], sm["tr"].n - 1 + half),
                          (sm["st_xnT"], 16 * s),
                          (sm["prj"], sm["prj"].n)]
                    if t == 0 and half == 0:
                        ew.append((sm["h2mm"], sm["h2mm"].n))
                    _op(P, "scalar", lambda e, t=t, half=half, trp=trp: e.copy(
                        out=xnT_sb.rearrange("p (c t2) -> p c t2", t2=T)
                            [:, half * 4:(half + 1) * 4, t * 128:(t + 1) * 128],
                        in_=trp.rearrange("p (c t2) -> p c t2", t2=128)),
                        waits=ew, inc=sm["xnT"])
                    _op(P, "scalar", lambda e, t=t, half=half, trp=trp: e.copy(
                        out=xnT32_rot[t % 2][:, half * 512:(half + 1) * 512],
                        in_=trp),
                        inc=sm["xnT32"])
                for c in range(8):
                    w = []
                    if c == 0:
                        w = [(sm["xnT32"], sm["xnT32"].n), (sm["ld_misc"], 48),
                             (sm["gv1"], sm["gv1"].n)]
                    _op(P, "tensor", lambda e, t=t, c=c: e.matmul(
                        psr[:, 0:8], xnT32_rot[t % 2][:, c * 128:(c + 1) * 128],
                        router_sb[:, c * 8:(c + 1) * 8],
                        start=(c == 0), stop=(c == 7)),
                        waits=w, inc=(sm["rmm"] if c == 7 else None))
                _op(P, "vector", lambda e: e.reduce_max(
                    out=gsc[:, 0:1], in_=psr[:, 0:8], axis=AX.X),
                    waits=[(sm["rmm"], sm["rmm"].n)], inc=sm["vch"])
                _op(P, "vector", lambda e: e.tensor_single_scalar(
                    out=gsc[:, 8:16], in_=psr[:, 0:8], scalar=gsc[:, 0:1],
                    op=ALU.subtract),
                    waits=[(sm["vch"], sm["vch"].n)],
                    inc=sm["gv1"])
                _op(P, "scalar", lambda e: e.activation(
                    out=gsc[:, 16:24], in_=gsc[:, 8:16], func=AF.Exp),
                    waits=[(sm["gv1"], sm["gv1"].n)], inc=sm["gs1"])
                _op(P, "vector", lambda e: e.tensor_single_scalar(
                    out=gsc[:, 24:32], in_=gsc[:, 8:16], scalar=0.0, op=ALU.is_ge),
                    waits=[(sm["gv1"], sm["gv1"].n)], inc=sm["vch"])
                _op(P, "vector", lambda e: e.scalar_tensor_tensor(
                    out=gsc[:, 32:40], in0=gsc[:, 24:32], scalar=-1e9,
                    in1=gsc[:, 8:16], op0=ALU.mult, op1=ALU.add),
                    waits=[(sm["vch"], sm["vch"].n)], inc=sm["vch"])
                _op(P, "vector", lambda e: e.reduce_max(
                    out=gsc[:, 1:2], in_=gsc[:, 32:40], axis=AX.X),
                    waits=[(sm["vch"], sm["vch"].n)], inc=sm["vch"])
                _op(P, "vector", lambda e: e.tensor_single_scalar(
                    out=gsc[:, 24:32], in_=gsc[:, 8:16], scalar=gsc[:, 1:2],
                    op=ALU.is_ge),
                    waits=[(sm["vch"], sm["vch"].n)], inc=sm["vch"])
                _op(P, "vector", lambda e: e.tensor_tensor(
                    out=gsc[:, 32:40], in0=gsc[:, 16:24], in1=gsc[:, 24:32],
                    op=ALU.mult),
                    waits=[(sm["gs1"], sm["gs1"].n), (sm["vch"], sm["vch"].n)],
                    inc=sm["vch"])
                _op(P, "vector", lambda e: e.reduce_sum(
                    out=gsc[:, 2:3], in_=gsc[:, 32:40], axis=AX.X),
                    waits=[(sm["vch"], sm["vch"].n)], inc=sm["vch"])
                _op(P, "vector", lambda e: e.reciprocal(
                    out=gsc[:, 3:4], in_=gsc[:, 2:3]),
                    waits=[(sm["vch"], sm["vch"].n)], inc=sm["vch"])
                _op(P, "vector", lambda e, t=t: e.tensor_single_scalar(
                    out=g_tok[:, t * 8:(t + 1) * 8], in_=gsc[:, 32:40],
                    scalar=gsc[:, 3:4], op=ALU.mult),
                    waits=[(sm["vch"], sm["vch"].n)],
                    inc=sm["gv2"])
                _op(P, "scalar", lambda e, t=t: e.copy(
                    out=gbf, in_=g_tok[:, t * 8:(t + 1) * 8]),
                    waits=[(sm["gv2"], sm["gv2"].n)], inc=sm["gs1"])
                _op(P, "tensor", lambda e: e.transpose(
                    psg[0:8, 0:128], gbf, ident_bf),
                    waits=[(sm["gs1"], sm["gs1"].n), (sm["gexp"], sm["gexp"].n)],
                    inc=sm["gtr"])
                _op(P, "scalar", lambda e, t=t: e.copy(
                    out=g_exp[0:8, t * 128:(t + 1) * 128], in_=psg[0:8, 0:128]),
                    waits=[(sm["gtr"], sm["gtr"].n), (sm["h2mm"], sm["h2mm"].n)],
                    inc=sm["gexp"])

            _op(P, "sync", lambda e: e.dma_start(
                out=s_xn[s].rearrange("c p t2 -> p c t2"),
                in_=xnT_sb.rearrange("p (c t2) -> p c t2", t2=T)),
                waits=[(sm["xnT"], sm["xnT"].n)], inc=sm["st_xnT"], inck=16)
            P.mark(f"LN{s}")

            ps_st = [P01, P23]
            ps_yt = P45
            ps_proj = P67[:, 0:512]
            ps_rlb = P67[:, 512:1024]

            _op(P, "sync", lambda e: e.dma_start(
                out=kvd_sb.rearrange("p (c j) -> p c j", j=128),
                in_=s_kvd.rearrange("(c p) j -> p c j", p=128)),
                waits=[(sm["st_wbf"], wbf_attn_done)], inc=sm["ld_attnw"], inck=16)
            _op(P, "sync", lambda e: e.dma_start(out=kvu_sb, in_=s_kvu),
                inc=sm["ld_attnw"], inck=16)
            _op(P, "sync", lambda e: e.dma_start(
                out=wq_sb.rearrange("p (c j) -> p c j", j=1024),
                in_=s_wq.rearrange("(c p) j -> p c j", p=128)),
                inc=sm["ld_attnw"], inck=16)
            _op(P, "sync", lambda e: e.dma_start(
                out=wo_sb.rearrange("p (h j) -> p h j", j=1024),
                in_=s_wo.rearrange("(h p) j -> p h j", p=128)),
                inc=sm["ld_attnw"], inck=16)
            attnw_done = 16 * 4 * (s + 1)
            drip(32)

            for g4 in range(4):
                for c in range(8):
                    w = []
                    if g4 == 0 and c == 0:
                        w = [(sm["ld_attnw"], attnw_done),
                             (sm["xnT"], sm["xnT"].n),
                             (sm["xnT32"], sm["xnT32"].n),
                             (sm["gv1"], sm["gv1"].n),
                             (sm["gexp"], sm["gexp"].n),
                             (sm["hev"], sm["hev"].n),
                             (sm["accv"], sm["accv"].n),
                             (sm["rl"], sm["rl"].n),
                             (sm["rlbe"], sm["rlbe"].n),
                             (sm["ynorm"], sm["ynorm"].n),
                             (sm["prjE"], sm["prjE"].n)]
                    elif c == 0:
                        w = [(sm["prjE"], sm["prjE"].n)]
                    _op(P, "tensor", lambda e, g4=g4, c=c: e.matmul(
                        ps_proj, kvd_sb[:, c * 128:(c + 1) * 128],
                        xnT_sb[:, c * T + g4 * 512: c * T + (g4 + 1) * 512],
                        start=(c == 0), stop=(c == 7)),
                        waits=w, inc=(sm["prj"] if c == 7 else None))
                lw = [(sm["prj"], sm["prj"].n)]
                if s > 0 and g4 == 0:
                    lw.append((sm["h2mm"], sm["h2mm"].n))
                _op(P, "scalar", lambda e, g4=g4: e.copy(
                    out=latentT[:, g4 * 512:(g4 + 1) * 512], in_=ps_proj),
                    waits=lw, inc=sm["prjE"])

            pending_rl = [None]

            def _emit_pv(p, hh, kc, g, expval, ynf, last=False):
                srot = kc % 2
                d = max(0, 128 * kc - 1024 * g)
                if d < 512:
                    wins = [(d, 512), (512, 1024)]
                else:
                    wins = [(d, 1024)]
                for wi, (a, b) in enumerate(wins):
                    w = [(sm["expo"], expval)]
                    if kc == 0 and wi == 0 and ynf:
                        w.append((sm["ynorm"], ynf))
                    lastwin = last and wi == len(wins) - 1
                    _op(P, "tensor",
                        lambda e, p=p, hh=hh, kc=kc, srot=srot, a=a, b=b,
                        st=(kc == 0), sp=lastwin: e.matmul(
                            ps_yt[0:65, a:b],
                            v_rot[p % 2][:, kc * 130 + hh * 65:
                                         kc * 130 + hh * 65 + 65],
                            pt_rot[srot][:, a:b],
                            start=st, stop=sp, skip_group_check=True),
                        waits=w,
                        inc=(sm["pv"] if wi == len(wins) - 1 else None))

            def _mk_rl(p, g, hh, bh):
                steps = []
                for half in range(2):
                    def s_recip(half=half):
                        def mk(e):
                            with nc.allow_low_precision(reason="bf16 softmax norm"):
                                return e.reciprocal(
                                    out=rl_bf[64:65, 0:512],
                                    in_=ps_yt[64:65, half * 512:(half + 1) * 512])
                        _op(P, "vector", mk,
                            waits=[(sm["pv"], sm["pv"].n),
                                   (sm["rlb"], sm["rlb"].n)],
                            inc=sm["rl"])
                    def s_bcast(half=half):
                        _op(P, "tensor", lambda e: e.matmul(
                            ps_rlb[0:64, 0:512], ones_bf[64:65, 0:64],
                            rl_bf[64:65, 0:512], start=True, stop=True),
                            waits=[(sm["rl"], sm["rl"].n),
                                   (sm["rlbe"], sm["rlbe"].n)],
                            inc=sm["rlb"])
                    def s_copy(half=half):
                        _op(P, "scalar", lambda e: e.copy(
                            out=rlb32, in_=ps_rlb[0:64, 0:512]),
                            waits=[(sm["rlb"], sm["rlb"].n),
                                   (sm["ynorm"], sm["ynorm"].n)],
                            inc=sm["rlbe"])
                    def s_ynorm(half=half):
                        _op(P, "vector",
                            lambda e, p=p, g=g, half=half, bh=bh:
                            e.tensor_tensor(
                                out=yT_all[bh:bh + 64,
                                           p * T + g * 1024 + half * 512:
                                           p * T + g * 1024 + (half + 1) * 512],
                                in0=ps_yt[0:64, half * 512:(half + 1) * 512],
                                in1=rlb32, op=ALU.mult),
                            waits=[(sm["rlbe"], sm["rlbe"].n)],
                            inc=sm["ynorm"])
                    steps += [s_recip, s_bcast, s_copy, s_ynorm]
                return steps

            for p in range(8):
                for g4 in range(4):
                    _op(P, "tensor", lambda e, p=p, g4=g4: e.matmul(
                        ps_proj, kvu_sb[:, p * 128:(p + 1) * 128],
                        latentT[:, g4 * 512:(g4 + 1) * 512],
                        start=True, stop=True),
                        waits=[(sm["prjE"], sm["prjE"].n)],
                        inc=sm["prj"])
                    _op(P, "scalar", lambda e, p=p, g4=g4: e.copy(
                        out=kT_rot[p % 2][:, g4 * 512:(g4 + 1) * 512],
                        in_=ps_proj),
                        waits=[(sm["prj"], sm["prj"].n)], inc=sm["prjE"])
                for g4 in range(4):
                    for c in range(8):
                        w = [(sm["prjE"], sm["prjE"].n)] if c == 0 else []
                        _op(P, "tensor", lambda e, p=p, g4=g4, c=c: e.matmul(
                            ps_proj,
                            wq_sb[:, c * 1024 + p * 128: c * 1024 + (p + 1) * 128],
                            xnT_sb[:, c * T + g4 * 512: c * T + (g4 + 1) * 512],
                            start=(c == 0), stop=(c == 7)),
                            waits=w, inc=(sm["prj"] if c == 7 else None))
                    _op(P, "scalar", lambda e, p=p, g4=g4: e.mul(
                        out=qT_rot[p % 2][:, g4 * 512:(g4 + 1) * 512],
                        in_=ps_proj, mul=0.125),
                        waits=[(sm["prj"], sm["prj"].n)], inc=sm["prjE"])
                mw = [(sm["pv"], sm["pv"].n), (sm["st_out"], sm["st_out"].n),
                      (sm["h2mm"], sm["h2mm"].n)]
                _op(P, "vector", lambda e, p=p: e.memset(
                    v_rot[p % 2].rearrange("p2 (k h j) -> p2 k h j", h=2, j=65)
                    [:, :, :, 64:65], 1.0),
                    waits=mw, inc=sm["vmem"])
                vmset = sm["vmem"].n
                for q4 in range(4):
                    for tq in range(4):
                        t = q4 * 4 + tq
                        w = [(sm["prjE"], sm["prjE"].n)] if tq == 0 else []
                        _op(P, "tensor", lambda e, p=p, t=t, tq=tq: e.matmul(
                            ps_proj[:, tq * 128:(tq + 1) * 128],
                            latentT[:, t * 128:(t + 1) * 128],
                            kvu_sb[:, 1024 + p * 128:1024 + (p + 1) * 128],
                            start=True, stop=True),
                            waits=w, inc=(sm["prj"] if tq == 3 else None))
                    _op(P, "vector", lambda e, p=p, q4=q4: e.tensor_copy(
                        out=v_rot[p % 2]
                            .rearrange("p2 (k h j) -> p2 k h j", h=2, j=65)
                            [:, q4 * 4:(q4 + 1) * 4, :, 0:64],
                        in_=ps_proj.rearrange("p2 (k h j) -> p2 k h j", h=2, j=64)),
                        waits=[(sm["prj"], sm["prj"].n),
                               (sm["cast"], vmset),
                               (sm["pv"], sm["pv"].n)],
                        inc=sm["prjE"])
                for hh in range(2):
                    bh = 64 * hh
                    for g in range(2):
                        drip(12)
                        nkc = 8 * (g + 1)
                        ynorm_final = sm["ynorm"].n + (2 if pending_rl[0] else 0)
                        expv = {}
                        for kc in range(nkc):
                            srot = kc % 2
                            d = max(0, 128 * kc - 1024 * g)
                            if d < 512:
                                wins = [(d, 512), (512, 1024)]
                            else:
                                wins = [(d, 1024)]
                            diag = 128 * kc >= 1024 * g
                            for wi, (a, b) in enumerate(wins):
                                w = []
                                if wi == 0:
                                    w = [(sm["prjE"], sm["prjE"].n),
                                         (sm["expo"], max(0, sm["expo"].n - 1))]
                                _op(P, "tensor",
                                    lambda e, p=p, kc=kc, g=g, a=a, b=b,
                                    srot=srot, bh=bh: e.matmul(
                                        ps_st[srot][:, a:b],
                                        kT_rot[p % 2][bh:bh + 64,
                                                      kc * 128:(kc + 1) * 128],
                                        qT_rot[p % 2][bh:bh + 64,
                                                      g * 1024 + a: g * 1024 + b],
                                        start=True, stop=True),
                                    waits=w,
                                    inc=(sm["qk"] if wi == len(wins) - 1 else None))
                            if diag:
                                _op(P, "vector", lambda e, srot=srot, d=d:
                                    e.tensor_tensor(
                                        out=ps_st[srot][:, d:d + 128],
                                        in0=ps_st[srot][:, d:d + 128],
                                        in1=mask_sb, op=ALU.add),
                                    waits=[(sm["qk"], sm["qk"].n)], inc=sm["msk"])
                            ew = [(sm["qk"], sm["qk"].n),
                                  (sm["pv"], sm["pv"].n)]
                            if diag:
                                ew.append((sm["msk"], sm["msk"].n))
                            _op(P, "scalar", lambda e, srot=srot, d=d: e.activation(
                                out=pt_rot[srot][:, d:1024],
                                in_=ps_st[srot][:, d:1024], func=AF.Exp),
                                waits=ew, inc=sm["expo"])
                            expv[kc] = sm["expo"].n
                            if kc <= 1 and pending_rl[0]:
                                steps = pending_rl[0][kc * 4:(kc + 1) * 4]
                                for st_fn in steps:
                                    st_fn()
                                if kc == 1:
                                    pending_rl[0] = None
                            if kc >= 1:
                                _emit_pv(p, hh, kc - 1, g, expv[kc - 1],
                                         ynorm_final if kc - 1 == 0 else None)
                        _emit_pv(p, hh, nkc - 1, g, expv[nkc - 1], None, last=True)
                        pending_rl[0] = _mk_rl(p, g, hh, bh)
            if pending_rl[0]:
                for st_fn in pending_rl[0]:
                    st_fn()
                pending_rl[0] = None

            ps_at = [P01, P23]
            for t in range(NTC):
                ldx_ct[0] += 1
                myx = 16 * ldx_ct[0]
                _op(P, "sync", lambda e, t=t: e.dma_start(
                    out=x_rot[t % 2], in_=d_x[s, t * 128:(t + 1) * 128, :]),
                    waits=[(sm["st_acc"], 16 * max(0, sm["st_acc"].n // 16 - 1)),
                           (sm["xn"], sm["xn"].n)],
                    inc=sm["ld_x"], inck=16)
                for pair in range(8):
                    for half in range(2):
                        w = []
                        if pair == 0 and half == 0:
                            w = [(sm["ynorm"], sm["ynorm"].n),
                                 (sm["expo"], sm["expo"].n),
                                 (sm["msk"], sm["msk"].n),
                                 (sm["gexp"], sm["gexp"].n),
                                 (sm["gv1"], sm["gv1"].n),
                                 (sm["accv"], max(0, sm["accv"].n - 1))]
                        _op(P, "tensor",
                            lambda e, t=t, pair=pair, half=half: e.matmul(
                                ps_at[t % 2][:, half * 512:(half + 1) * 512],
                                yT_all[:, pair * T + t * 128: pair * T + (t + 1) * 128],
                                wo_sb[:, pair * 1024 + half * 512:
                                      pair * 1024 + (half + 1) * 512],
                                start=(pair == 0), stop=(pair == 7)),
                            waits=w,
                            inc=(sm["wo"] if pair == 7 and half == 1 else None))
                _op(P, "vector", lambda e, t=t: e.scalar_tensor_tensor(
                    out=x_rot[t % 2], in0=ps_at[t % 2], scalar=1.0,
                    in1=x_rot[t % 2], op0=ALU.mult, op1=ALU.add),
                    waits=[(sm["wo"], sm["wo"].n), (sm["ld_x"], myx)],
                    inc=sm["accv"])
                _op(P, "sync", lambda e, t=t: e.dma_start(
                    out=s_acc[s, t * 128:(t + 1) * 128, :], in_=x_rot[t % 2]),
                    waits=[(sm["accv"], sm["accv"].n)], inc=sm["st_acc"], inck=16)
                drip(8)
            if debug:
                _op(P, "sync", lambda e: e.dma_start(
                    out=d_gtok[s], in_=g_tok),
                    waits=[(sm["gv2"], sm["gv2"].n)], inc=sm["st_acc"], inck=16)

        def seq_phase_B(s):
            ps_h1 = [P01, P23]
            ps_h2 = [P45, P67]
            for r in range(NSTR):
                Rg = s * NSTR + r
                _op(P, "sync", lambda e, r=r: e.dma_start(
                    out=xn_str.rearrange("p (c t2) -> p c t2", t2=STRIPE),
                    in_=s_xn[s][:, :, r * STRIPE:(r + 1) * STRIPE]
                        .rearrange("c p t2 -> p c t2")),
                    waits=[(sm["st_xnT"], 16 * (s + 1)),
                           (sm["h1f"], sm["h1f"].n),
                           (sm["st_acc"], 16 * 16 * (s + 1)),
                           (sm["accv"], 16 * (s + 1)),
                           (sm["wo"], 16 * (s + 1))],
                    inc=sm["ld_xn"], inck=16)
                _op(P, "sync", lambda e, r=r: e.dma_start(
                    out=acc_sb.rearrange("p (tc c2) -> p tc c2", c2=1024),
                    in_=s_acc[s, r * STRIPE:(r + 1) * STRIPE, :]
                        .rearrange("(tc p) c2 -> p tc c2", p=128)),
                    waits=[(sm["st_out"], 16 * Rg),
                           (sm["rlbe"], sm["rlbe"].n)],
                    inc=sm["ld_acc"], inck=16)
                for tc in range(8):
                    for half in range(2):
                        w = []
                        if half == 0:
                            w = [(sm["gexp"], 16 * (s + 1)),
                                 (sm["cast"], CAST0),
                                 (sm["hev"], max(0, sm["hev"].n - 1))]
                            if tc == 0:
                                w += [(sm["ynorm"], sm["ynorm"].n),
                                      (sm["rl"], sm["rl"].n),
                                      (sm["rlbe"], sm["rlbe"].n),
                                      (sm["prjE"], sm["prjE"].n)]
                        _op(P, "tensor", lambda e, r=r, tc=tc, half=half: e.matmul(
                            ps_h2[tc % 2][:, half * 512:(half + 1) * 512],
                            g_exp[0:8, (r * 8 + tc) * 128:(r * 8 + tc + 1) * 128],
                            b2_bf[0:8, half * 512:(half + 1) * 512],
                            start=True, stop=True),
                            waits=w, inc=(sm["h2mm"] if half == 1 else None))
                    _op(P, "vector", lambda e, tc=tc, Rg=Rg: e.scalar_tensor_tensor(
                        out=acc_sb[:, tc * 1024:(tc + 1) * 1024],
                        in0=ps_h2[tc % 2], scalar=1.0,
                        in1=acc_sb[:, tc * 1024:(tc + 1) * 1024],
                        op0=ALU.mult, op1=ALU.add),
                        waits=[(sm["h2mm"], sm["h2mm"].n),
                               (sm["ld_acc"], 16 * (Rg + 1))],
                        inc=sm["hev"])
                for ex in range(E):
                    _op(P, "sync", lambda e, ex=ex: e.dma_start(
                        out=b1_cur[:, (ex % 2) * 32:(ex % 2) * 32 + 32],
                        in_=d_b1[ex]),
                        waits=[(sm["gelu"], max(0, sm["gelu"].n - 32))],
                        inc=sm["ld_b1"], inck=16)
                    b1done = 16 * (Rg * 8 + ex + 1)
                    w2done = None
                    for f in range(32):
                        slot = glob_slab[0] % 4
                        glob_slab[0] += 1
                        myslab = glob_slab[0]
                        _op(P, "sync", lambda e, ex=ex, f=f, slot=slot: e.dma_start(
                            out=w1_ring[:, slot * 1024:(slot + 1) * 1024]
                                .rearrange("p (k j) -> p k j", j=128),
                            in_=s_w1[ex][:, f * 128:(f + 1) * 128]
                                .rearrange("(k p) j -> p k j", p=128)),
                            waits=[(sm["st_wbf"], wbf_w1_done[ex]),
                                   (sm["h1f"], max(0, myslab - 4))],
                            inc=sm["ld_w1"], inck=16)
                        if f == 7:
                            for q in range(4):
                                _op(P, "sync", lambda e, ex=ex, q=q: e.dma_start(
                                    out=w2_sb.rearrange("p (f2 j) -> p f2 j", j=1024)
                                        [:, q * 8:(q + 1) * 8, :],
                                    in_=s_w2[ex].rearrange("(f2 p) j -> p f2 j", p=128)
                                        [:, q * 8:(q + 1) * 8, :]),
                                    waits=[(sm["st_wbf"], wbf_w2_done[ex]),
                                           (sm["h2mm"], sm["h2mm"].n)],
                                    inc=sm["ld_w2"], inck=16)
                            w2done = 16 * ((Rg * 8 + ex + 1) * 4)
                            drip(32)
                        for g2 in range(2):
                            for k in range(8):
                                w = []
                                if g2 == 0 and k == 0:
                                    w = [(sm["ld_w1"], 16 * myslab),
                                         (sm["ld_xn"], 16 * (Rg + 1)),
                                         (sm["gelu"], max(0, sm["gelu"].n - 1))]
                                    if f == 0 and ex == 0:
                                        w.append((sm["accv"], sm["accv"].n))
                                _op(P, "tensor",
                                    lambda e, f=f, g2=g2, k=k, slot=slot: e.matmul(
                                        ps_h1[f % 2][:, g2 * 512:(g2 + 1) * 512],
                                        w1_ring[:, slot * 1024 + k * 128:
                                                slot * 1024 + (k + 1) * 128],
                                        xn_str[:, k * STRIPE + g2 * 512:
                                               k * STRIPE + (g2 + 1) * 512],
                                        start=(k == 0), stop=(k == 7)),
                                    waits=w,
                                    inc=(sm["h1f"] if g2 == 1 and k == 7 else None))
                        _op(P, "scalar", lambda e, ex=ex, f=f: e.activation(
                            out=h1T[:, f * STRIPE:(f + 1) * STRIPE],
                            in_=ps_h1[f % 2], func=AF.Gelu,
                            bias=b1_cur[:, (ex % 2) * 32 + f:
                                        (ex % 2) * 32 + f + 1]),
                            waits=[(sm["h1f"], sm["h1f"].n),
                                   (sm["ld_b1"], b1done)],
                            inc=sm["gelu"])
                    geludone = sm["gelu"].n
                    for tc in range(8):
                        for f in range(32):
                            for half in range(2):
                                w = []
                                if f == 0 and half == 0:
                                    w = [(sm["gelu"], geludone),
                                         (sm["ld_w2"], w2done),
                                         (sm["hev"], max(0, sm["hev"].n - 1))]
                                _op(P, "tensor",
                                    lambda e, tc=tc, f=f, half=half: e.matmul(
                                        ps_h2[tc % 2][:, half * 512:(half + 1) * 512],
                                        h1T[:, f * STRIPE + tc * 128:
                                            f * STRIPE + (tc + 1) * 128],
                                        w2_sb[:, f * 1024 + half * 512:
                                              f * 1024 + (half + 1) * 512],
                                        start=(f == 0), stop=(f == 31)),
                                    waits=w,
                                    inc=(sm["h2mm"] if f == 31 and half == 1
                                         else None))
                        _op(P, "vector",
                            lambda e, r=r, tc=tc, ex=ex: e.scalar_tensor_tensor(
                                out=acc_sb[:, tc * 1024:(tc + 1) * 1024],
                                in0=ps_h2[tc % 2],
                                scalar=g_tok[:, (r * 8 + tc) * 8 + ex:
                                             (r * 8 + tc) * 8 + ex + 1],
                                in1=acc_sb[:, tc * 1024:(tc + 1) * 1024],
                                op0=ALU.mult, op1=ALU.add),
                            waits=[(sm["h2mm"], sm["h2mm"].n)],
                            inc=sm["hev"])
                _op(P, "sync", lambda e, s2=s, r=r: e.dma_start(
                    out=d_out[s2, r * STRIPE:(r + 1) * STRIPE, :]
                        .rearrange("(tc p) c2 -> p tc c2", p=128),
                    in_=acc_sb.rearrange("p (tc c2) -> p tc c2", c2=1024)),
                    waits=[(sm["hev"], sm["hev"].n)], inc=sm["st_out"], inck=16)

        P.mark("prepass")
        for s in range(NSEQ):
            seq_phase_A(s)
            P.mark(f"A{s}")
            seq_phase_B(s)
            P.mark(f"B{s}")

        drip(1 << 30)
        _op(P, "gpsimd", lambda e: e.wait_ge(sm["st_out"].h, 16 * NSEQ * NSTR))
        _op(P, "sync", lambda e: e.wait_ge(sm["st_out"].h, 16 * NSEQ * NSTR))

        if upto and str(upto).startswith("@"):
            cut = int(str(upto)[1:])
        else:
            cut = P.marks.get(upto, None) if upto else None
        build_info["marks"] = dict(P.marks)
        build_info["labels"] = list(P.labels)

        def _run(eng, key):
            for i, f in P.streams[key]:
                if cut is None or i <= cut:
                    f(eng)

        with nc.Block() as block:
            @block.sync
            def _(eng):
                _run(eng, "sync")

            @block.tensor
            def _(eng):
                _run(eng, "tensor")

            @block.vector
            def _(eng):
                _run(eng, "vector")

            @block.scalar
            def _(eng):
                _run(eng, "scalar")

            @block.gpsimd
            def _(eng):
                _run(eng, "gpsimd")

    return nc


def _host_inputs(L0_attn_x, L0_attn_w_kv_down, L0_attn_w_kv_up, L0_attn_w_q,
                 L0_attn_w_out, L0_ffn_router_w, L0_ffn_w1, L0_ffn_b1,
                 L0_ffn_w2, L0_ffn_b2):
    x = np.ascontiguousarray(np.asarray(L0_attn_x, dtype=np.float32))
    w_kv_down = np.ascontiguousarray(np.asarray(L0_attn_w_kv_down, dtype=np.float32))
    w_kv_up = np.ascontiguousarray(np.asarray(L0_attn_w_kv_up, dtype=np.float32))
    w_q = np.ascontiguousarray(np.asarray(L0_attn_w_q, dtype=np.float32))
    w_out = np.ascontiguousarray(np.asarray(L0_attn_w_out, dtype=np.float32))
    router_wt = np.ascontiguousarray(np.asarray(L0_ffn_router_w, dtype=np.float32).T)
    w1t = np.ascontiguousarray(np.asarray(L0_ffn_w1, dtype=np.float32).transpose(0, 2, 1))
    w2t = np.ascontiguousarray(np.asarray(L0_ffn_w2, dtype=np.float32).transpose(0, 2, 1))
    b1t = np.ascontiguousarray(
        np.asarray(L0_ffn_b1, dtype=np.float32).reshape(E, 32, 128).transpose(0, 2, 1))
    b2 = np.ascontiguousarray(np.asarray(L0_ffn_b2, dtype=np.float32))
    ident = np.eye(128, dtype=np.float32)
    kk, qq = np.meshgrid(np.arange(128), np.arange(128), indexing="ij")
    mask = np.where(qq >= kk, 0.0, -30000.0).astype(np.float32)
    shared = dict(w_kv_down=w_kv_down, w_kv_up=w_kv_up, w_q=w_q, w_out=w_out,
                  router_wt=router_wt, w1t=w1t, w2t=w2t, b1t=b1t, b2=b2,
                  ident=ident, mask_tri=mask)
    in_maps = []
    for c in range(8):
        m = dict(shared)
        m["x"] = np.ascontiguousarray(x[c * NSEQ:(c + 1) * NSEQ])
        in_maps.append(m)
    return in_maps


def kernel(**inputs):
    in_maps = _host_inputs(**inputs)
    nc = build_program(debug=DEBUG)
    res = run_bass_kernel_spmd(nc, in_maps, core_ids=list(range(8)))
    out = np.concatenate([r["out"] for r in res.results], axis=0)
    return out.astype(np.float32)
